# version 03
# baseline (speedup 1.0000x reference)
import numpy as np
import ml_dtypes
from contextlib import ExitStack

import concourse.bass as bass
import concourse.mybir as mybir
from concourse.bass_utils import run_bass_kernel_spmd

F32 = mybir.dt.float32
BF16 = mybir.dt.bfloat16
AF = mybir.ActivationFunctionType
ALU = mybir.AluOpType
AX = mybir.AxisListType

T = 4096
E = 1024
NH = 4
DH = 256
FF = 2048
NE = 8
NDC = E // 128
NFC = FF // 128
NTC = T // 128
NTS = T // 512
NEG = -3.0e38

EPS_OUT = 1e-5
EPS_MHN = 1e-6
EPS_FFN = 1e-6
EPS_N = 1e-6
CAP = 15.0

SCRATCH = {
    "qT_hi": ([NDC, 128, T], BF16), "qT_lo": ([NDC, 128, T], BF16),
    "kT_hi": ([NDC, 128, T], BF16), "kT_lo": ([NDC, 128, T], BF16),
    "v_hi": ([NTC, 128, E], BF16), "v_lo": ([NTC, 128, E], BF16),
    "sigo_d": ([NTC, 128, E], F32),
    "gates_d": ([2 * NH, T], F32),
    "a_d": ([NH, T], F32), "negm_d": ([NH, T], F32), "em_d": ([NH, T], F32),
    "rs_d": ([NH, T], F32), "scan_tmp": ([2, 128], F32),
    "dbg": ([8, 512], F32),
    "ho_d": ([NTC, 128, E], F32),
    "x2_d": ([NTC, 128, E], F32),
    "x2T_hi": ([NDC, 128, T], BF16), "x2T_lo": ([NDC, 128, T], BF16),
    "sg_d": ([NE, T], F32),
    "hT_hi": ([NFC, 128, T], BF16), "hT_lo": ([NFC, 128, T], BF16),
    "acc_d": ([NTC, 128, E], F32),
}


class MK:
    def __init__(self, debug_outs=(), phases="ABCDE"):
        self.phases = phases
        self.debug_outs = set(debug_outs)
        self.nc = bass.Bass("TRN2", target_bir_lowering=False, debug=False,
                            detect_race_conditions=False)
        self.d = {}
        self.sems = {}
        self.es = ExitStack()

    def sem(self, name):
        if name not in self.sems:
            self.sems[name] = [self.es.enter_context(self.nc.semaphore(name)), 0]
        return self.sems[name]

    def inc(self, name, inst, k=1):
        h = self.sem(name)
        inst.then_inc(h[0], k)
        h[1] += k
        return h[1]

    def dinc(self, name, inst):
        return self.inc(name, inst, 16)

    def val(self, name):
        return self.sem(name)[1]

    def wait(self, eng, name, v=None):
        h = self.sem(name)
        v = h[1] if v is None else v
        if v > 0:
            eng.wait_ge(h[0], v)

    def vfence(self, eng, ins, tag="v_f"):
        n = self.inc(tag, ins)
        self.wait(eng, tag, n)

    def fence(self, eng):
        for name in list(self.sems):
            self.wait(eng, name)

    def barrier(self, block):
        for engn in ("sync", "tensor", "scalar", "vector", "gpsimd"):
            getattr(block, engn)(lambda e: self.fence(e))

    def declare_dram(self):
        nc, d = self.nc, self.d
        ins = {
            "x": [T, E], "wq": [E, E], "wk": [E, E], "wv": [E, E],
            "wo": [E, E], "wdown": [E, E], "wgates": [E, 2 * NH],
            "bgates": [2 * NH, 1], "normw_rep": [128, E], "gate_w": [E, NE],
            "w1": [NE, E, FF], "b1c": [NE, 128, NFC], "w2": [NE, FF, E],
            "b2_rep": [NE, 128, E], "g_rep": [NE, 128, E],
            "bln_rep": [NE, 128, E], "identf": [128, 128], "tri": [128, 128],
            "ones_row": [1, 128],
        }
        for k, sh in ins.items():
            d[k] = nc.dram_tensor(k, sh, F32, kind="ExternalInput").ap()
        d["identb"] = nc.dram_tensor("identb", [128, 128], BF16, kind="ExternalInput").ap()
        d["ones_colb"] = nc.dram_tensor("ones_colb", [128, 1], BF16, kind="ExternalInput").ap()
        d["out"] = nc.dram_tensor("out", [T, E], F32, kind="ExternalOutput").ap()
        for k, (sh, dt) in SCRATCH.items():
            kind = "ExternalOutput" if k in self.debug_outs else "Internal"
            d[k] = nc.dram_tensor(k, sh, dt, kind=kind).ap()

    def sbuf(self, stack, name, shape, dt=F32):
        t = stack.enter_context(self.nc.sbuf_tensor(name, shape, dt))
        return t.ap()

    def psum(self, stack, name, shape, dt=F32):
        t = stack.enter_context(self.nc.psum_tensor(name, shape, dt))
        return t.ap()

    def build(self):
        self.declare_dram()
        nc = self.nc
        with self.es:
            cs = {}
            for k, dt in (("identf", F32), ("identb", BF16), ("tri", F32),
                          ("ones_row", F32), ("ones_colb", BF16),
                          ("normw_rep", F32)):
                shape = {"ones_row": [1, 128], "ones_colb": [128, 1],
                         "normw_rep": [128, E]}.get(k, [128, 128])
                cs[k] = self.sbuf(self.es, "c_" + k, shape, dt)
            cs["bg15"] = self.sbuf(self.es, "c_bg15", [2 * NH, 1], F32)
            cs["wEb"] = self.sbuf(self.es, "c_wEb", [128, NTC, NE], F32)
            self.cs = cs
            with nc.Block() as block:
                self.load_consts(block)
                if "A" in self.phases:
                    self.phase_a(block)
                if "B" in self.phases or "P" in self.phases:
                    self.barrier(block)
                    self.phase_b_pre(block)
                    if "B" in self.phases:
                        self.phase_b(block)
                    self.barrier(block)
                    self.b_stack.close()
                if "C" in self.phases:
                    self.barrier(block)
                    self.phase_c(block)
                    self.phase_gating(block)
                if "D" in self.phases:
                    for e in range(NE):
                        self.barrier(block)
                        self.phase_d1(block, e)
                        self.barrier(block)
                        self.phase_d2(block, e)
                if "E" in self.phases:
                    self.barrier(block)
                    self.phase_e(block)
                self.finalize(block)
        return nc

    def load_consts(self, block):
        d, cs = self.d, self.cs

        def sy(s):
            for k in ("identf", "identb", "tri", "ones_row", "ones_colb",
                      "normw_rep"):
                self.dinc("s_c", s.dma_start(out=cs[k][:], in_=d[k][:]))
            self.dinc("s_c", s.dma_start(out=cs["bg15"][:], in_=d["bgates"][:]))
        block.sync(sy)

        def ve(v):
            self.wait(v, "s_c")
            self.inc("v_c", v.tensor_scalar_mul(cs["bg15"][:], cs["bg15"][:], 1.0 / CAP))
        block.vector(ve)

    def phase_a(self, block):
        nc, d, cs = self.nc, self.d, self.cs
        with ExitStack() as st:
            xT_hi = self.sbuf(st, "a_xT_hi", [128, NDC, T // 2], BF16)
            xT_lo = self.sbuf(st, "a_xT_lo", [128, NDC, T // 2], BF16)
            w_f32 = self.sbuf(st, "a_w_f32", [128, NDC, E], F32)
            w_hi = self.sbuf(st, "a_w_hi", [128, NDC, E], BF16)
            w_lo = self.sbuf(st, "a_w_lo", [128, NDC, E], BF16)
            xtm = self.sbuf(st, "a_xtm", [128, 2, 4, E], F32)
            xtm_hi = self.sbuf(st, "a_xtm_hi", [128, 4, E], BF16)
            xtm_lo = self.sbuf(st, "a_xtm_lo", [128, 4, E], BF16)
            ev_a = self.sbuf(st, "a_ev_a", [128, 2, 2, 512], F32)
            ev_hi = self.sbuf(st, "a_ev_hi", [128, 2, 2, 512], BF16)
            ev_lo = self.sbuf(st, "a_ev_lo", [128, 2, 2, 512], BF16)
            grow = self.sbuf(st, "a_grow", [2 * NH, 512], F32)
            ps_mm = self.psum(st, "a_ps_mm", [128, 4, 512], F32)
            ps_tp = self.psum(st, "a_ps_tp", [128, 4, 8, 128], BF16)

            tp_slot_free = [("v_c", 0)] * 8
            mm_slot_free = [("v_c", 0)] * 2
            ev_slot_free = [("v_c", 0)] * 2

            for half in range(2):
                t0 = half * (T // 2)
                for ls in range(4):
                    ts = half * 4 + ls
                    xb = ts % 2

                    def sy(s, ts=ts, xb=xb):
                        self.wait(s, "v_xsp", self.val("v_xsp"))
                        self.dinc("s_x", s.dma_start(
                            out=xtm[:, xb],
                            in_=d["x"][ts * 512:(ts + 1) * 512, :]
                                .rearrange("(c p) d -> p c d", p=128)))
                    block.sync(sy)

                    def ve(v, ts=ts, xb=xb):
                        self.wait(v, "s_x")
                        self.wait(v, "t_tp", self.val("t_tp"))
                        v.tensor_copy(xtm_hi[:], xtm[:, xb])
                        self.inc("v_xsp", v.tensor_sub(xtm_lo[:], xtm[:, xb], xtm_hi[:]))
                    block.vector(ve)

                    def te(t, ls=ls):
                        self.wait(t, "v_xsp")
                        for c in range(4):
                            for dc in range(NDC):
                                for hl, src in ((0, xtm_hi), (1, xtm_lo)):
                                    g = self.val("t_tp")
                                    self.wait(t, "v_tpe", g - 3)
                                    ins = t.transpose(
                                        ps_tp[:, g % 4, (g // 4) % 8],
                                        src[:, c, dc * 128:(dc + 1) * 128],
                                        cs["identb"])
                                    self.inc("t_tp", ins)
                    block.tensor(te)

                    def ve2(v, ls=ls):
                        self.wait(v, "t_pj", self.val("t_pj"))
                        cnt = self.val("t_tp") - 64
                        for k in range(64):
                            c, r = divmod(k, 16)
                            dc, hl = divmod(r, 2)
                            dst = xT_hi if hl == 0 else xT_lo
                            g = cnt + k
                            self.wait(v, "t_tp", g + 1)
                            col = ls * 512 + c * 128
                            ins = v.tensor_copy(
                                dst[:, dc, col:col + 128],
                                ps_tp[:, g % 4, (g // 4) % 8])
                            self.inc("v_tpe", ins)
                    block.vector(ve2)

                for wname in ("wq", "wk", "wgates", "wv", "wo"):
                    wcols = 2 * NH if wname == "wgates" else E

                    def sy(s, wname=wname, wcols=wcols):
                        self.wait(s, "t_pj", self.val("t_pj"))
                        self.dinc("s_w", s.dma_start(
                            out=w_f32[:, :, 0:wcols],
                            in_=d[wname].rearrange("(c p) o -> p c o", p=128)))
                    block.sync(sy)

                    def ve(v, wcols=wcols):
                        self.wait(v, "s_w")
                        v.tensor_copy(w_hi[:, :, 0:wcols], w_f32[:, :, 0:wcols])
                        self.inc("v_w", v.tensor_sub(
                            w_lo[:, :, 0:wcols], w_f32[:, :, 0:wcols],
                            w_hi[:, :, 0:wcols]))
                    block.vector(ve)

                    if wname in ("wq", "wk"):
                        groups = [("fm", mc, ls) for mc in range(NDC) for ls in range(4)]
                    elif wname == "wgates":
                        groups = [("g", 0, ls) for ls in range(4)]
                    else:
                        groups = [("tm", tc, 0) for tc in range(16)]
                    for gi, (kind, aa, bb) in enumerate(groups):
                        slot = gi % 2

                        def te(t, kind=kind, aa=aa, bb=bb, slot=slot, wname=wname):
                            self.wait(t, "v_w")
                            sm, sv = mm_slot_free[slot]
                            self.wait(t, sm, sv)
                            ins = None
                            if kind == "fm":
                                ops = [(pw, px, dc) for (pw, px) in
                                       ((w_hi, xT_hi), (w_hi, xT_lo), (w_lo, xT_hi))
                                       for dc in range(NDC)]
                                for i, (pw, px, dc) in enumerate(ops):
                                    ins = t.matmul(
                                        ps_mm[:, slot * 2],
                                        pw[:, dc, aa * 128:(aa + 1) * 128],
                                        px[:, dc, bb * 512:(bb + 1) * 512],
                                        start=(i == 0), stop=(i == len(ops) - 1))
                            elif kind == "g":
                                ops = [(pw, px, dc) for (pw, px) in
                                       ((w_hi, xT_hi), (w_hi, xT_lo), (w_lo, xT_hi))
                                       for dc in range(NDC)]
                                for i, (pw, px, dc) in enumerate(ops):
                                    ins = t.matmul(
                                        ps_mm[0:2 * NH, slot * 2],
                                        pw[:, dc, 0:2 * NH],
                                        px[:, dc, bb * 512:(bb + 1) * 512],
                                        start=(i == 0), stop=(i == len(ops) - 1))
                            else:
                                ops = [(px, pw, dc, ns) for (px, pw) in
                                       ((xT_hi, w_hi), (xT_hi, w_lo), (xT_lo, w_hi))
                                       for dc in range(NDC) for ns in range(2)]
                                for i, (px, pw, dc, ns) in enumerate(ops):
                                    ins = t.matmul(
                                        ps_mm[:, slot * 2 + ns],
                                        px[:, dc, aa * 128:(aa + 1) * 128],
                                        pw[:, dc, ns * 512:(ns + 1) * 512],
                                        start=(i < 2), stop=(i >= len(ops) - 2))
                            self.inc("t_pj", ins)
                        block.tensor(te)

                        tok = (half * 16 + aa) if kind == "tm" else None

                        if wname in ("wq", "wk"):
                            def ve2(v, aa=aa, bb=bb, slot=slot, wname=wname, half=half):
                                self.wait(v, "t_pj")
                                sm, sv = ev_slot_free[slot]
                                self.wait(v, sm, sv)
                                sc = (1.0 / 16.0) if wname == "wq" else 1.0
                                ps = ps_mm[:, slot * 2]
                                v.tensor_scalar_mul(ev_hi[:, slot, 0], ps, sc)
                                ins = v.scalar_tensor_tensor(
                                    ev_lo[:, slot, 0], ps, sc, ev_hi[:, slot, 0],
                                    ALU.mult, ALU.subtract)
                                self.inc("v_pj", ins)
                            block.vector(ve2)

                            def ge(g, aa=aa, bb=bb, slot=slot, wname=wname, half=half):
                                self.wait(g, "v_pj")
                                pre = "qT" if wname == "wq" else "kT"
                                col = half * 2048 + bb * 512
                                i1 = g.dma_start(out=d[pre + "_hi"][aa, :, col:col + 512],
                                                 in_=ev_hi[:, slot, 0])
                                self.dinc("g_sp", i1)
                                i2 = g.dma_start(out=d[pre + "_lo"][aa, :, col:col + 512],
                                                 in_=ev_lo[:, slot, 0])
                                self.dinc("g_sp", i2)
                                ev_slot_free[slot] = ("g_sp", self.val("g_sp"))
                            block.gpsimd(ge)
                        elif wname == "wgates":
                            def se(sc_, bb=bb, slot=slot, half=half):
                                self.wait(sc_, "t_pj")
                                self.wait(sc_, "g_sp", self.val("g_sp"))
                                ins = sc_.activation(
                                    grow[:], ps_mm[0:2 * NH, slot * 2], AF.Tanh,
                                    bias=self.cs["bg15"][:], scale=1.0 / CAP)
                                self.inc("s_ev", ins)
                            block.scalar(se)

                            def ve2(v, bb=bb, half=half):
                                self.wait(v, "s_ev")
                                ins = v.tensor_scalar_mul(grow[:], grow[:], CAP)
                                self.inc("v_pj", ins)
                            block.vector(ve2)

                            def ge(g, bb=bb, half=half):
                                self.wait(g, "v_pj")
                                col = half * 2048 + bb * 512
                                ins = g.dma_start(out=d["gates_d"][:, col:col + 512],
                                                  in_=grow[:])
                                self.dinc("g_sp", ins)
                            block.gpsimd(ge)
                        elif wname == "wv":
                            def ve2(v, slot=slot):
                                self.wait(v, "t_pj")
                                sm, sv = ev_slot_free[slot]
                                self.wait(v, sm, sv)
                                for ns in range(2):
                                    ps = ps_mm[:, slot * 2 + ns]
                                    v.tensor_copy(ev_hi[:, slot, ns], ps)
                                    ins = v.tensor_sub(ev_lo[:, slot, ns], ps,
                                                       ev_hi[:, slot, ns])
                                self.inc("v_pj", ins)
                            block.vector(ve2)

                            def ge(g, tok=tok, slot=slot):
                                self.wait(g, "v_pj")
                                i1 = g.dma_start(
                                    out=d["v_hi"][tok],
                                    in_=ev_hi[:, slot].rearrange("p a b -> p (a b)"))
                                self.dinc("g_sp", i1)
                                i2 = g.dma_start(
                                    out=d["v_lo"][tok],
                                    in_=ev_lo[:, slot].rearrange("p a b -> p (a b)"))
                                self.dinc("g_sp", i2)
                                ev_slot_free[slot] = ("g_sp", self.val("g_sp"))
                            block.gpsimd(ge)
                        else:
                            def se(sc_, slot=slot):
                                self.wait(sc_, "t_pj")
                                sm, sv = ev_slot_free[slot]
                                self.wait(sc_, sm, sv)
                                for ns in range(2):
                                    ins = sc_.activation(
                                        ev_a[:, slot, ns], ps_mm[:, slot * 2 + ns],
                                        AF.Sigmoid)
                                self.inc("s_ev", ins)
                            block.scalar(se)

                            def ge(g, tok=tok, slot=slot):
                                self.wait(g, "s_ev")
                                ins = g.dma_start(
                                    out=d["sigo_d"][tok],
                                    in_=ev_a[:, slot].rearrange("p a b -> p (a b)"))
                                ev_slot_free[slot] = ("g_sp", self.dinc("g_sp", ins))
                            block.gpsimd(ge)

                        if kind == "fm":
                            mm_slot_free[slot] = ("v_pj", self.val("v_pj"))
                        elif kind == "g":
                            mm_slot_free[slot] = ("s_ev", self.val("s_ev"))
                        else:
                            mm_slot_free[slot] = (
                                ("v_pj", self.val("v_pj")) if wname == "wv"
                                else ("s_ev", self.val("s_ev")))

    def phase_b_pre(self, block):
        nc, d, cs = self.nc, self.d, self.cs
        st = self.b_stack = ExitStack()
        sc = {}
        for n in ("ig", "fg", "ls", "pA", "pB", "lf", "a", "M", "m", "em"):
            sc[n] = self.sbuf(st, "b_sc_" + n, [128, 128], F32)
        r0 = self.sbuf(st, "b_r0", [1, 128], F32)
        r1 = self.sbuf(st, "b_r1", [1, 128], F32)
        colp = self.sbuf(st, "b_colp", [128, 1], F32)
        a_blk = self.sbuf(st, "b_a_blk", [128, 128], F32)
        em_blk = self.sbuf(st, "b_em_blk", [128, 128], F32)
        self.b_sc, self.b_ablk, self.b_emblk = sc, a_blk, em_blk

        class R:
            def __init__(self, ap, last):
                self.ap, self.last = ap, last

            def __getitem__(self, i):
                return self.ap[i]

        def scan(v, src, dst, op):
            bufs = [sc["pA"], sc["pB"]]
            cur = src
            last = None
            for i, k in enumerate((1, 2, 4, 8, 16, 32, 64)):
                nxt = bufs[i % 2] if i < 6 else dst
                if last is not None:
                    self.vfence(v, last)
                v.tensor_copy(nxt[:, :k], cur[:, :k])
                last = v.tensor_tensor(nxt[:, k:], cur[:, k:],
                                       cur[:, :128 - k], op)
                cur = nxt
            return R(cur, last)

        def rowscan(v, op):
            bufs = [r1, r0]
            cur = r0
            last = None
            for i, k in enumerate((1, 2, 4, 8, 16)):
                nxt = bufs[i % 2]
                if last is not None:
                    self.vfence(v, last)
                for h in range(NH):
                    s0 = h * 32
                    v.tensor_copy(nxt[:, s0:s0 + k], cur[:, s0:s0 + k])
                    last = v.tensor_tensor(
                        nxt[:, s0 + k:s0 + 32], cur[:, s0 + k:s0 + 32],
                        cur[:, s0:s0 + 32 - k], op)
                cur = nxt
            return R(cur, last)

        def sy(s):
            self.dinc("s_bp", s.dma_start(
                out=sc["ig"][:],
                in_=d["gates_d"][0:NH].rearrange("h (b t) -> (h b) t", b=NTC)))
            self.dinc("s_bp", s.dma_start(
                out=sc["fg"][:],
                in_=d["gates_d"][NH:2 * NH].rearrange("h (b t) -> (h b) t", b=NTC)))
        block.sync(sy)

        def se(s):
            self.wait(s, "s_bp")
            s.activation(sc["ls"][:], sc["fg"][:], AF.Sigmoid)
            self.inc("s_bpe", s.activation(sc["ls"][:], sc["ls"][:], AF.Ln))
        block.scalar(se)

        def ve(v):
            self.wait(v, "s_bpe")
            fin = scan(v, sc["ls"], sc["lf"], ALU.add)
            self.vfence(v, fin.last)
            self.inc("v_bp", v.tensor_copy(colp[:], fin[:, 127:128]))
        block.vector(ve)

        def ge(g):
            self.wait(g, "v_bp")
            self.dinc("g_dbg", g.dma_start(
                out=d["dbg"][4:5, 0:128].rearrange("a b -> b a"), in_=colp[:]))
            self.dinc("g_bp", g.dma_start(
                out=d["scan_tmp"][0:1, :].rearrange("a b -> b a"), in_=colp[:]))
            g.wait_ge(self.sem("g_bp")[0], self.val("g_bp"))
            self.dinc("g_dbg", g.dma_start(
                out=d["dbg"][5:6, 0:128], in_=d["scan_tmp"][0:1, :]))
        block.gpsimd(ge)

        def sy2(s):
            self.wait(s, "g_bp")
            self.dinc("s_bp", s.dma_start(out=r0[:], in_=d["scan_tmp"][0:1, :]))
        block.sync(sy2)

        def gdb1(g):
            self.wait(g, "s_bp")
            self.dinc("g_dbg", g.dma_start(out=d["dbg"][0:1, 0:128], in_=r0[:]))
            self.dinc("g_dbg", g.dma_start(
                out=d["rs_d"].rearrange("h (b t) -> (h b) t", b=NTC),
                in_=sc["lf"][:]))
            self.dinc("g_dbg", g.dma_start(
                out=d["em_d"].rearrange("h (b t) -> (h b) t", b=NTC),
                in_=sc["ls"][:]))
        block.gpsimd(gdb1)

        def ve2(v):
            self.wait(v, "s_bp")
            self.wait(v, "g_dbg", self.val("g_dbg"))
            fin = rowscan(v, ALU.add)
            self.vfence(v, fin.last)
            other = r0 if fin.ap is r1 else r1
            ins = None
            for h in range(NH):
                s0 = h * 32
                v.memset(other[:, s0:s0 + 1], 0.0)
                ins = v.tensor_copy(other[:, s0 + 1:s0 + 32],
                                    fin[:, s0:s0 + 31])
            self.vfence(v, ins)
            self.inc("v_bp", v.tensor_copy(r1[:, 0:1], r1[:, 0:1]))
        block.vector(ve2)

        def ge2(g):
            self.wait(g, "v_bp")
            self.dinc("g_dbg", g.dma_start(out=d["dbg"][1:2, 0:128], in_=r0[:]))
            self.dinc("g_dbg", g.dma_start(out=d["dbg"][2:3, 0:128], in_=r1[:]))
            self.dinc("g_bp", g.dma_start(
                out=d["scan_tmp"][1:2, :].rearrange("a b -> b a"), in_=r0[:]))
        block.gpsimd(ge2)

        def sy3(s):
            self.wait(s, "g_bp")
            self.dinc("s_bp", s.dma_start(
                out=colp[:], in_=d["scan_tmp"][1:2, :].rearrange("a b -> b a")))
        block.sync(sy3)

        def gdb2(g):
            self.wait(g, "s_bp")
            self.dinc("g_dbg", g.dma_start(
                out=d["dbg"][3:4, 0:128].rearrange("a b -> b a"), in_=colp[:]))
        block.gpsimd(gdb2)

        def ve3(v):
            self.wait(v, "s_bp")
            self.wait(v, "g_dbg", self.val("g_dbg"))
            ins = v.tensor_scalar_add(sc["lf"][:], sc["lf"][:], colp[:])
            self.vfence(v, ins)
            ins = v.tensor_sub(sc["a"][:], sc["ig"][:], sc["lf"][:])
            self.vfence(v, ins)
            fin = scan(v, sc["a"], sc["M"], ALU.max)
            self.vfence(v, fin.last)
            self.inc("v_bp", v.tensor_copy(colp[:], sc["M"][:, 127:128]))
        block.vector(ve3)

        def ge3(g):
            self.wait(g, "v_bp")
            self.dinc("g_bp", g.dma_start(
                out=d["scan_tmp"][0:1, :].rearrange("a b -> b a"), in_=colp[:]))
        block.gpsimd(ge3)

        def sy4(s):
            self.wait(s, "g_bp")
            self.dinc("s_bp", s.dma_start(out=r0[:], in_=d["scan_tmp"][0:1, :]))
        block.sync(sy4)

        def ve4(v):
            self.wait(v, "s_bp")
            fin = rowscan(v, ALU.max)
            self.vfence(v, fin.last)
            other = r0 if fin.ap is r1 else r1
            ins = None
            for h in range(NH):
                s0 = h * 32
                v.memset(other[:, s0:s0 + 1], NEG)
                ins = v.tensor_copy(other[:, s0 + 1:s0 + 32],
                                    fin[:, s0:s0 + 31])
            self.vfence(v, ins)
            self.inc("v_bp", v.tensor_copy(r1[:, 0:1], r1[:, 0:1]))
        block.vector(ve4)

        def ge4(g):
            self.wait(g, "v_bp")
            self.dinc("g_bp", g.dma_start(
                out=d["scan_tmp"][1:2, :].rearrange("a b -> b a"), in_=r0[:]))
        block.gpsimd(ge4)

        def sy5(s):
            self.wait(s, "g_bp")
            self.dinc("s_bp", s.dma_start(
                out=colp[:], in_=d["scan_tmp"][1:2, :].rearrange("a b -> b a")))
        block.sync(sy5)

        def ve5(v):
            self.wait(v, "s_bp")
            ins = v.tensor_scalar(sc["M"][:], sc["M"][:], colp[:], None, ALU.max)
            self.vfence(v, ins)
            v.tensor_add(sc["m"][:], sc["lf"][:], sc["M"][:])
            self.inc("v_bp", v.tensor_scalar_mul(sc["M"][:], sc["M"][:], -1.0))
        block.vector(ve5)

        def se2(s):
            self.wait(s, "v_bp")
            self.inc("s_bpe", s.activation(sc["em"][:], sc["m"][:], AF.Exp,
                                           scale=-1.0))
        block.scalar(se2)

        def ge5(g):
            self.wait(g, "s_bpe")
            self.wait(g, "v_bp")
            self.dinc("g_bp", g.dma_start(
                out=d["a_d"].rearrange("h (b t) -> (h b) t", b=NTC),
                in_=sc["a"][:]))
            self.dinc("g_bp", g.dma_start(
                out=d["negm_d"].rearrange("h (b t) -> (h b) t", b=NTC),
                in_=sc["M"][:]))
            self.dinc("g_bp", g.dma_start(
                out=d["em_d"].rearrange("h (b t) -> (h b) t", b=NTC),
                in_=sc["em"][:]))
        block.gpsimd(ge5)

        with ExitStack() as pst:
            ps_bp = self.psum(pst, "b_ps_bp", [128, 4, 128], F32)

            def te_bp(t):
                self.wait(t, "s_bpe")
                self.wait(t, "v_bp")
                t.transpose(ps_bp[:, 0], sc["a"][:], cs["identf"])
                self.inc("t_bp", t.transpose(ps_bp[:, 1], sc["em"][:],
                                             cs["identf"]))
            block.tensor(te_bp)

            def ve_bp(v):
                self.wait(v, "t_bp")
                v.tensor_copy(a_blk[:], ps_bp[:, 0])
                self.inc("v_abk", v.tensor_copy(em_blk[:], ps_bp[:, 1]))
            block.vector(ve_bp)

    def phase_b(self, block):
        nc, d, cs = self.nc, self.d, self.cs
        st = self.b_stack
        a_blk, em_blk = self.b_ablk, self.b_emblk
        qh_hi = self.sbuf(st, "b_qh_hi", [128, 2, T], BF16)
        qh_lo = self.sbuf(st, "b_qh_lo", [128, 2, T], BF16)
        kh_hi = self.sbuf(st, "b_kh_hi", [128, 2, T], BF16)
        kh_lo = self.sbuf(st, "b_kh_lo", [128, 2, T], BF16)
        vh_hi = self.sbuf(st, "b_vh_hi", [128, NTC, DH], BF16)
        vh_lo = self.sbuf(st, "b_vh_lo", [128, NTC, DH], BF16)
        sig_sb = self.sbuf(st, "b_sig", [128, 2, 4, DH], F32)
        nmrow = self.sbuf(st, "b_nmrow", [1, NTS, 512], F32)
        dt_sb = self.sbuf(st, "b_dt", [128, 4, 512], F32)
        cfull = self.sbuf(st, "b_cfull", [128, 4, 512], F32)
        chi = self.sbuf(st, "b_chi", [128, 4, 512], BF16)
        clo = self.sbuf(st, "b_clo", [128, 4, 512], BF16)
        h_sb = self.sbuf(st, "b_h", [128, 4, DH], F32)
        hsq = self.sbuf(st, "b_hsq", [128, 4, DH], F32)
        hout = self.sbuf(st, "b_hout", [128, 4, DH], F32)
        rs_cols = self.sbuf(st, "b_rs_cols", [128, 2, 4], F32)
        stt = self.sbuf(st, "b_stats", [128, 8, 4], F32)
        ps_s = self.psum(st, "b_ps_s", [128, 4, 512], F32)
        ps_repl = self.psum(st, "b_ps_repl", [128, 512], F32)
        ps_h = self.psum(st, "b_ps_h", [128, 4, DH], F32)
        ps_rs = self.psum(st, "b_ps_rs", [128, 512], F32)

        gpair = [0]

        for h in range(NH):
            def syh(s, h=h):
                self.wait(s, "t_cv", self.val("t_cv"))
                self.wait(s, "t_sc", self.val("t_sc"))
                self.wait(s, "t_repl", self.val("t_repl"))
                for dst, src in ((qh_hi, d["qT_hi"]), (qh_lo, d["qT_lo"]),
                                 (kh_hi, d["kT_hi"]), (kh_lo, d["kT_lo"])):
                    self.dinc("s_qkv", s.dma_start(
                        out=dst[:],
                        in_=src[2 * h:2 * h + 2].rearrange("c p t -> p c t")))
                for dst, src in ((vh_hi, d["v_hi"]), (vh_lo, d["v_lo"])):
                    self.dinc("s_qkv", s.dma_start(
                        out=dst[:],
                        in_=src[:, :, h * DH:(h + 1) * DH]
                            .rearrange("t p d -> p t d")))
                self.dinc("s_qkv", s.dma_start(
                    out=nmrow[:],
                    in_=d["negm_d"][h:h + 1, :]
                        .rearrange("o (a b) -> o a b", b=512)))
            block.sync(syh)

            pending = []

            for it in range(NTS):
                njc = 4 * it + 4

                def sc1(s, h=h, it=it):
                    self.wait(s, "v_mhn", self.val("v_mhn"))
                    for ic in range(4):
                        self.dinc("s_sg", s.dma_start(
                            out=sig_sb[:, it % 2, ic],
                            in_=d["sigo_d"][it * 4 + ic, :,
                                            h * DH:(h + 1) * DH]))
                block.scalar(sc1)

                def te1(t, h=h, it=it):
                    self.wait(t, "s_qkv")
                    self.wait(t, "v_abk")
                    self.wait(t, "s_exp", self.val("s_exp"))
                    ins = t.matmul(ps_repl[:], cs["ones_row"],
                                   nmrow[0:1, it, :], start=True, stop=True)
                    self.inc("t_repl", ins)
                block.tensor(te1)

                for jp in range(njc // 2):
                    jcs = (2 * jp, 2 * jp + 1)
                    psl = gpair[0] % 2
                    offs = [max(0, jc * 128 - it * 512) for jc in jcs]
                    nvs = [512 - o for o in offs]

                    def te2(t, h=h, it=it, jcs=jcs, psl=psl, offs=offs,
                            nvs=nvs):
                        self.wait(t, "v_c", self.val("v_c") - 1)
                        if jcs[0] == 0:
                            self.wait(t, "v_he", self.val("v_he"))
                        ops = [(kh_hi, qh_hi), (kh_hi, qh_lo), (kh_lo, qh_hi)]
                        for j01, jc in enumerate(jcs):
                            off, nv = offs[j01], nvs[j01]
                            for i, (kk, qq) in enumerate(ops):
                                for dc in range(2):
                                    ins = t.matmul(
                                        ps_s[:, psl * 2 + j01, 0:nv],
                                        kk[:, dc, jc * 128:(jc + 1) * 128],
                                        qq[:, dc, it * 512 + off:(it + 1) * 512],
                                        start=(i == 0 and dc == 0),
                                        stop=(i == 2 and dc == 1))
                        self.inc("t_sc", ins)
                    block.tensor(te2)

                    def se1(s, h=h, it=it, jcs=jcs, psl=psl, offs=offs,
                            nvs=nvs):
                        self.wait(s, "t_sc")
                        self.wait(s, "t_repl")
                        self.wait(s, "s_bp")
                        self.wait(s, "v_c", self.val("v_c") - 1)
                        for j01, jc in enumerate(jcs):
                            ins = s.activation(
                                dt_sb[:, psl * 2 + j01, 0:nvs[j01]],
                                ps_repl[:, offs[j01]:512], AF.Exp,
                                bias=a_blk[:, h * 32 + jc:h * 32 + jc + 1])
                        self.inc("s_exp", ins)
                    block.scalar(se1)

                    def ve1(v, h=h, it=it, jcs=jcs, psl=psl, offs=offs,
                            nvs=nvs):
                        self.wait(v, "s_exp")
                        for j01, jc in enumerate(jcs):
                            sl = psl * 2 + j01
                            off, nv = offs[j01], nvs[j01]
                            if jc >= 4 * it:
                                v.tensor_mul(dt_sb[:, sl, 0:128],
                                             dt_sb[:, sl, 0:128], cs["tri"][:])
                            v.tensor_mul(cfull[:, sl, 0:nv], ps_s[:, sl, 0:nv],
                                         dt_sb[:, sl, 0:nv])
                            v.tensor_copy(chi[:, sl, 0:nv],
                                          cfull[:, sl, 0:nv])
                            ins = v.tensor_sub(clo[:, sl, 0:nv],
                                               cfull[:, sl, 0:nv],
                                               chi[:, sl, 0:nv])
                        self.inc("v_c", ins)
                    block.vector(ve1)

                    def te3(t, h=h, it=it, jcs=jcs, psl=psl, offs=offs,
                            nvs=nvs, njc=njc):
                        self.wait(t, "v_c")
                        for j01, jc in enumerate(jcs):
                            sl = psl * 2 + j01
                            off = offs[j01]
                            for ic in range(off // 128, 4):
                                gic = 4 * it + ic
                                for cc, vv, pi in ((chi, vh_hi, 0),
                                                   (chi, vh_lo, 1),
                                                   (clo, vh_hi, 2)):
                                    t.matmul(
                                        ps_h[:, ic],
                                        cc[:, sl, ic * 128 - off:
                                           ic * 128 - off + 128],
                                        vv[:, jc],
                                        start=(jc == 0 and pi == 0
                                               and ic % 2 == 0),
                                        stop=(jc == gic and pi == 2))
                            for ic in range(off // 128, 4):
                                gic = 4 * it + ic
                                sli = slice(ic * 128 - off, ic * 128 - off + 128)
                                t.matmul(ps_rs[:, ic:ic + 1], chi[:, sl, sli],
                                         cs["ones_colb"],
                                         start=(jc == 0 and ic == 0),
                                         stop=False)
                                ins = t.matmul(ps_rs[:, ic:ic + 1],
                                               clo[:, sl, sli],
                                               cs["ones_colb"],
                                               start=False, stop=(jc == gic))
                        self.inc("t_cv", ins)
                    block.tensor(te3)

                    gpair[0] += 1

                    if jp == 1 and pending:
                        for fn in pending:
                            fn()
                        pending = []

                def ve2(v, h=h, it=it):
                    self.wait(v, "t_cv")
                    for ic in range(4):
                        v.tensor_copy(h_sb[:, ic], ps_h[:, ic])
                    v.tensor_copy(rs_cols[:, it % 2], ps_rs[:, 0:4])
                    v.tensor_mul(hsq[:], h_sb[:], h_sb[:])
                    v.tensor_reduce(stt[:, 0, :], h_sb[:], AX.X, ALU.add)
                    ins = v.tensor_reduce(stt[:, 1, :], hsq[:], AX.X, ALU.add)
                    self.inc("v_he", ins)
                block.vector(ve2)

                def mk_tail(h=h, it=it, ssg_val=None):
                    ssg_val = self.val("s_sg")

                    def ve3(v):
                        self.wait(v, "s_sg", ssg_val)
                        self.wait(v, "v_he")
                        rc = rs_cols[:, it % 2]
                        f = lambda i: self.vfence(v, i)
                        f(v.tensor_scalar_mul(stt[:, 2, :], rc, -1.0))
                        f(v.tensor_tensor(stt[:, 2, :], rc, stt[:, 2, :],
                                          ALU.max))
                        f(v.tensor_tensor(
                            stt[:, 2, :], stt[:, 2, :],
                            em_blk[:, h * 32 + it * 4:h * 32 + it * 4 + 4],
                            ALU.max))
                        f(v.tensor_scalar_add(stt[:, 2, :], stt[:, 2, :],
                                              EPS_N))
                        f(v.tensor_mul(stt[:, 2, :], stt[:, 2, :],
                                       stt[:, 2, :]))
                        v.tensor_scalar_mul(stt[:, 3, :], stt[:, 0, :],
                                            1.0 / DH)
                        f(v.tensor_scalar_mul(stt[:, 4, :], stt[:, 1, :],
                                              1.0 / DH))
                        f(v.tensor_mul(stt[:, 5, :], stt[:, 3, :],
                                       stt[:, 3, :]))
                        f(v.tensor_sub(stt[:, 4, :], stt[:, 4, :],
                                       stt[:, 5, :]))
                        ins = v.scalar_tensor_tensor(
                            stt[:, 4, :], stt[:, 2, :], EPS_MHN, stt[:, 4, :],
                            ALU.mult, ALU.add)
                        self.inc("v_mhn1", ins)

                    def se2(s):
                        self.wait(s, "v_mhn1")
                        ins = s.activation(stt[:, 5, :], stt[:, 4, :], AF.Sqrt)
                        self.inc("s_mhn", ins)

                    def ve4(v):
                        self.wait(v, "s_mhn")
                        self.wait(v, "g_ho", self.val("g_ho"))
                        f = lambda i: self.vfence(v, i)
                        f(v.reciprocal(stt[:, 6, :], stt[:, 5, :]))
                        f(v.tensor_mul(stt[:, 7, :], stt[:, 6, :],
                                       stt[:, 6, :]))
                        f(v.tensor_mul(stt[:, 7, :], stt[:, 7, :],
                                       stt[:, 4, :]))
                        f(v.tensor_scalar(stt[:, 7, :], stt[:, 7, :], -0.5,
                                          1.5, ALU.mult, ALU.add))
                        f(v.tensor_mul(stt[:, 6, :], stt[:, 6, :],
                                       stt[:, 7, :]))
                        f(v.tensor_mul(stt[:, 5, :], stt[:, 3, :],
                                       stt[:, 6, :]))
                        f(v.tensor_scalar_mul(stt[:, 5, :], stt[:, 5, :],
                                              -1.0))
                        for ic in range(4):
                            v.tensor_scalar(
                                hout[:, ic], h_sb[:, ic],
                                stt[:, 6, ic:ic + 1], stt[:, 5, ic:ic + 1],
                                ALU.mult, ALU.add)
                            v.tensor_mul(hout[:, ic], hout[:, ic],
                                         cs["normw_rep"][:,
                                                         h * DH:(h + 1) * DH])
                            ins = v.tensor_mul(hout[:, ic], hout[:, ic],
                                               sig_sb[:, it % 2, ic])
                        self.inc("v_mhn", ins)

                    def ge2(g):
                        self.wait(g, "v_mhn")
                        for ic in range(4):
                            self.dinc("g_ho", g.dma_start(
                                out=d["ho_d"][it * 4 + ic, :,
                                              h * DH:(h + 1) * DH],
                                in_=hout[:, ic]))

                    def emit():
                        block.vector(ve3)
                        block.scalar(se2)
                        block.vector(ve4)
                        block.gpsimd(ge2)
                    return emit

                pending.append(mk_tail(h, it))

            for fn in pending:
                fn()
            pending = []

    def phase_c(self, block):
        nc, d, cs = self.nc, self.d, self.cs
        with ExitStack() as st:
            wd_f32 = self.sbuf(st, "c_wd_f32", [128, NDC, E], F32)
            wd_hi = self.sbuf(st, "c_wd_hi", [128, NDC, E], BF16)
            wd_lo = self.sbuf(st, "c_wd_lo", [128, NDC, E], BF16)
            ho_tm = self.sbuf(st, "c_ho_tm", [128, 2, E], F32)
            ho_tmh = self.sbuf(st, "c_ho_tmh", [128, 2, E], BF16)
            ho_tml = self.sbuf(st, "c_ho_tml", [128, 2, E], BF16)
            ho_hi = self.sbuf(st, "c_ho_hi", [128, 2, NDC, 128], BF16)
            ho_lo = self.sbuf(st, "c_ho_lo", [128, 2, NDC, 128], BF16)
            xsl = self.sbuf(st, "c_xsl", [128, 2, E], F32)
            rbuf = self.sbuf(st, "c_r", [128, E], F32)
            rsq = self.sbuf(st, "c_rsq", [128, E], F32)
            x2t = self.sbuf(st, "c_x2t", [128, 2, E], F32)
            x2hi = self.sbuf(st, "c_x2hi", [128, E], BF16)
            x2lo = self.sbuf(st, "c_x2lo", [128, E], BF16)
            stg_hi = self.sbuf(st, "c_stg_hi", [128, NDC, 128], BF16)
            stg_lo = self.sbuf(st, "c_stg_lo", [128, NDC, 128], BF16)
            cols = self.sbuf(st, "c_cols", [128, 8], F32)
            ps_o = self.psum(st, "c_ps_o", [128, 2, 2, 512], F32)
            ps_tp = self.psum(st, "c_ps_tp", [128, 4, 8, 128], BF16)

            def syw(s):
                self.dinc("s_wd", s.dma_start(
                    out=wd_f32[:],
                    in_=d["wdown"].rearrange("(c p) o -> p c o", p=128)))
            block.sync(syw)

            def vew(v):
                self.wait(v, "s_wd")
                v.tensor_copy(wd_hi[:], wd_f32[:])
                self.inc("v_wd", v.tensor_sub(wd_lo[:], wd_f32[:], wd_hi[:]))
            block.vector(vew)

            for tc in range(NTC):
                sl = tc % 2

                def sy1(s, tc=tc, sl=sl):
                    self.wait(s, "v_hs", self.val("v_hs") - 1)
                    self.dinc("s_ho", s.dma_start(
                        out=ho_tm[:, sl], in_=d["ho_d"][tc]))
                    self.wait(s, "v_c2", self.val("v_c2"))
                    self.dinc("s_ho", s.dma_start(
                        out=xsl[:, sl],
                        in_=d["x"][tc * 128:(tc + 1) * 128, :]))
                block.sync(sy1)

                def vsp(v, tc=tc, sl=sl):
                    self.wait(v, "s_ho")
                    self.wait(v, "t_tph", self.val("t_tph"))
                    v.tensor_copy(ho_tmh[:, sl], ho_tm[:, sl])
                    ins = v.tensor_sub(ho_tml[:, sl], ho_tm[:, sl],
                                       ho_tmh[:, sl])
                    self.inc("v_hs", ins)
                block.vector(vsp)

                def ttp(t, tc=tc, sl=sl):
                    self.wait(t, "v_hs")
                    for dc in range(NDC):
                        for hl, srcb in ((0, ho_tmh), (1, ho_tml)):
                            g = self.val("t_tph") + self.val("t_tp3")
                            self.wait(t, "v_tpe3", g - 3)
                            gg = g % 4
                            ins = t.transpose(
                                ps_tp[:, gg, ((g // 4) % 8)],
                                srcb[:, sl, dc * 128:(dc + 1) * 128],
                                cs["identb"])
                            self.inc("t_tph", ins)
                block.tensor(ttp)

                def vtpe(v, tc=tc, sl=sl):
                    self.wait(v, "t_c", self.val("t_c"))
                    base = self.val("t_tph") + self.val("t_tp3") - 16
                    for k in range(16):
                        dc, hl = divmod(k, 2)
                        dst = ho_hi if hl == 0 else ho_lo
                        g = base + k
                        self.wait(v, "t_tph", self.val("t_tph") - 16 + k + 1)
                        ins = v.tensor_copy(dst[:, sl, dc],
                                            ps_tp[:, g % 4, (g // 4) % 8])
                        self.inc("v_tpe3", ins)
                block.vector(vtpe)

                def te1(t, tc=tc, sl=sl):
                    self.wait(t, "v_tpe3")
                    self.wait(t, "v_wd")
                    self.wait(t, "v_c2", self.val("v_c2") - 1)
                    ins = None
                    ops = [(ho_hi, wd_hi), (ho_hi, wd_lo), (ho_lo, wd_hi)]
                    for ns in range(2):
                        for i, (hh, ww) in enumerate(ops):
                            for dc in range(NDC):
                                ins = t.matmul(
                                    ps_o[:, sl, ns], hh[:, sl, dc],
                                    ww[:, dc, ns * 512:(ns + 1) * 512],
                                    start=(i == 0 and dc == 0),
                                    stop=(i == 2 and dc == NDC - 1))
                    self.inc("t_c", ins)
                block.tensor(te1)

                def ve1(v, tc=tc, sl=sl):
                    self.wait(v, "t_c")
                    f = lambda i: self.vfence(v, i)
                    psv = ps_o[:, sl].rearrange("p a b -> p (a b)")
                    f(v.tensor_add(rbuf[:], psv, xsl[:, sl]))
                    v.tensor_mul(rsq[:], rbuf[:], rbuf[:])
                    v.tensor_reduce(cols[:, 0:1], rbuf[:], AX.X, ALU.add)
                    f(v.tensor_reduce(cols[:, 1:2], rsq[:], AX.X, ALU.add))
                    f(v.tensor_scalar_mul(cols[:, 0:1], cols[:, 0:1], 1.0 / E))
                    f(v.tensor_scalar_mul(cols[:, 1:2], cols[:, 1:2], 1.0 / E))
                    f(v.tensor_mul(cols[:, 2:3], cols[:, 0:1], cols[:, 0:1]))
                    f(v.tensor_sub(cols[:, 1:2], cols[:, 1:2], cols[:, 2:3]))
                    ins = v.tensor_scalar_add(cols[:, 1:2], cols[:, 1:2],
                                              EPS_OUT)
                    self.inc("v_ln1", ins)
                block.vector(ve1)

                def se1(s, tc=tc):
                    self.wait(s, "v_ln1")
                    ins = s.activation(cols[:, 3:4], cols[:, 1:2], AF.Sqrt)
                    self.inc("s_ln", ins)
                block.scalar(se1)

                def ve2(v, tc=tc, sl=sl):
                    self.wait(v, "s_ln")
                    f = lambda i: self.vfence(v, i)
                    f(v.reciprocal(cols[:, 4:5], cols[:, 3:4]))
                    f(v.tensor_mul(cols[:, 5:6], cols[:, 4:5], cols[:, 4:5]))
                    f(v.tensor_mul(cols[:, 5:6], cols[:, 5:6], cols[:, 1:2]))
                    f(v.tensor_scalar(cols[:, 5:6], cols[:, 5:6], -0.5, 1.5,
                                      ALU.mult, ALU.add))
                    f(v.tensor_mul(cols[:, 4:5], cols[:, 4:5], cols[:, 5:6]))
                    f(v.tensor_mul(cols[:, 6:7], cols[:, 0:1], cols[:, 4:5]))
                    f(v.tensor_scalar_mul(cols[:, 6:7], cols[:, 6:7], -1.0))
                    v.tensor_scalar(x2t[:, sl], rbuf[:], cols[:, 4:5],
                                    cols[:, 6:7], ALU.mult, ALU.add)
                    self.wait(v, "t_tp3", self.val("t_tp3"))
                    v.tensor_copy(x2hi[:], x2t[:, sl])
                    ins = v.tensor_sub(x2lo[:], x2t[:, sl], x2hi[:])
                    self.inc("v_c2", ins)
                block.vector(ve2)

                def te2(t, tc=tc):
                    self.wait(t, "v_c2")
                    for dc in range(NDC):
                        for hl, srcb in ((0, x2hi), (1, x2lo)):
                            g = self.val("t_tph") + self.val("t_tp3")
                            self.wait(t, "v_tpe3", g - 3)
                            ins = t.transpose(
                                ps_tp[:, g % 4, (g // 4) % 8],
                                srcb[:, dc * 128:(dc + 1) * 128], cs["identb"])
                            self.inc("t_tp3", ins)
                block.tensor(te2)

                def ve3(v, tc=tc):
                    self.wait(v, "g_c", self.val("g_c"))
                    base = self.val("t_tph") + self.val("t_tp3") - 16
                    for k in range(16):
                        dc, hl = divmod(k, 2)
                        dst = stg_hi if hl == 0 else stg_lo
                        g = base + k
                        self.wait(v, "t_tp3", self.val("t_tp3") - 16 + k + 1)
                        ins = v.tensor_copy(dst[:, dc],
                                            ps_tp[:, g % 4, (g // 4) % 8])
                        self.inc("v_tpe3", ins)
                block.vector(ve3)

                def ge1(g, tc=tc, sl=sl):
                    self.wait(g, "v_tpe3")
                    self.dinc("g_c", g.dma_start(out=d["x2_d"][tc],
                                                 in_=x2t[:, sl]))
                    self.dinc("g_c", g.dma_start(
                        out=d["x2T_hi"][:, :, tc * 128:(tc + 1) * 128]
                            .rearrange("c p t -> p c t"),
                        in_=stg_hi[:]))
                    self.dinc("g_c", g.dma_start(
                        out=d["x2T_lo"][:, :, tc * 128:(tc + 1) * 128]
                            .rearrange("c p t -> p c t"),
                        in_=stg_lo[:]))
                block.gpsimd(ge1)

    def phase_gating(self, block):
        nc, d, cs = self.nc, self.d, self.cs
        with ExitStack() as st:
            gw_f32 = self.sbuf(st, "g_gw_f32", [128, NDC, NE], F32)
            gw_hi = self.sbuf(st, "g_gw_hi", [128, NDC, NE], BF16)
            gw_lo = self.sbuf(st, "g_gw_lo", [128, NDC, NE], BF16)
            xsl_hi = self.sbuf(st, "g_x_hi", [128, 2, NDC, 512], BF16)
            xsl_lo = self.sbuf(st, "g_x_lo", [128, 2, NDC, 512], BF16)
            sg_sb = self.sbuf(st, "g_sg", [128, 512], F32)
            sgb = self.sbuf(st, "g_sgb", [128, NTC, NE], F32)
            m1 = self.sbuf(st, "g_m1", [128, NTC, 1], F32)
            m2 = self.sbuf(st, "g_m2", [128, NTC, 1], F32)
            eq = self.sbuf(st, "g_eq", [128, NTC, NE], F32)
            cum = self.sbuf(st, "g_cum", [128, 2, NTC, NE], F32)
            msk1 = self.sbuf(st, "g_msk1", [128, NTC, NE], F32)
            msk2 = self.sbuf(st, "g_msk2", [128, NTC, NE], F32)
            sg2 = self.sbuf(st, "g_sg2", [128, NTC, NE], F32)
            w12 = self.sbuf(st, "g_w12", [128, 2, NTC], F32)
            ps_g = self.psum(st, "g_ps", [128, 2, 512], F32)
            ps_t = self.psum(st, "g_pst", [128, 4, 128], F32)

            def syw(s):
                self.wait(s, "g_c")
                self.dinc("s_gw", s.dma_start(
                    out=gw_f32[:],
                    in_=d["gate_w"].rearrange("(c p) o -> p c o", p=128)))
            block.sync(syw)

            def vew(v):
                self.wait(v, "s_gw")
                v.tensor_copy(gw_hi[:], gw_f32[:])
                self.inc("v_gw", v.tensor_sub(gw_lo[:], gw_f32[:], gw_hi[:]))
            block.vector(vew)

            for ts in range(NTS):
                sl = ts % 2

                def sy1(s, ts=ts, sl=sl):
                    self.wait(s, "t_g", self.val("t_g"))
                    self.dinc("s_gx", s.dma_start(
                        out=xsl_hi[:, sl],
                        in_=d["x2T_hi"][:, :, ts * 512:(ts + 1) * 512]
                            .rearrange("c p t -> p c t")))
                    self.dinc("s_gx", s.dma_start(
                        out=xsl_lo[:, sl],
                        in_=d["x2T_lo"][:, :, ts * 512:(ts + 1) * 512]
                            .rearrange("c p t -> p c t")))
                block.sync(sy1)

                def te1(t, ts=ts, sl=sl):
                    self.wait(t, "s_gx")
                    self.wait(t, "v_gw")
                    self.wait(t, "v_g1", self.val("v_g1") - 1)
                    ins = None
                    ops = [(gw_hi, xsl_hi), (gw_hi, xsl_lo), (gw_lo, xsl_hi)]
                    for i, (ww, xx) in enumerate(ops):
                        for dc in range(NDC):
                            ins = t.matmul(
                                ps_g[0:NE, sl], ww[:, dc], xx[:, sl, dc],
                                start=(i == 0 and dc == 0),
                                stop=(i == 2 and dc == NDC - 1))
                    self.inc("t_g", ins)
                block.tensor(te1)

                def ve1(v, ts=ts, sl=sl):
                    self.wait(v, "t_g")
                    self.wait(v, "t_g2", self.val("t_g2"))
                    ins = v.tensor_copy(sg_sb[0:NE, :], ps_g[0:NE, sl])
                    self.inc("v_g1", ins)
                block.vector(ve1)

                def te2(t, ts=ts):
                    self.wait(t, "v_g1")
                    ins = None
                    for bi in range(4):
                        self.wait(t, "v_g2", self.val("v_g2"))
                        ins = t.transpose(ps_t[:, bi, 0:NE],
                                          sg_sb[0:NE, bi * 128:(bi + 1) * 128],
                                          cs["identf"][0:NE, 0:NE])
                        self.inc("t_g2", ins)
                block.tensor(te2)

                def ve2(v, ts=ts):
                    self.wait(v, "t_g2")
                    ins = None
                    for bi in range(4):
                        ins = v.tensor_copy(sgb[:, ts * 4 + bi, :],
                                            ps_t[:, bi, 0:NE])
                    self.inc("v_g2", ins)
                block.vector(ve2)

            def vtop1(v):
                self.wait(v, "v_g2")
                f = lambda i: self.vfence(v, i)
                bc = lambda t3: t3.broadcast_to([128, NTC, NE])
                f(v.tensor_reduce(m1[:, :, 0], sgb[:], AX.X, ALU.max))
                f(v.tensor_tensor(eq[:], sgb[:], bc(m1[:]), ALU.is_ge))
                cur = eq
                for i, k in enumerate((1, 2, 4)):
                    nxt = cum[:, i % 2]
                    v.tensor_copy(nxt[:, :, 0:k], cur[:, :, 0:k])
                    ins = v.tensor_tensor(nxt[:, :, k:], cur[:, :, k:],
                                          cur[:, :, :NE - k], ALU.add)
                    f(ins)
                    cur = nxt
                f(v.tensor_sub(cum[:, 1], cur[:], eq[:]))
                f(v.tensor_single_scalar(cum[:, 0], cum[:, 1], 0.0,
                                         ALU.is_equal))
                f(v.tensor_mul(msk1[:], eq[:], cum[:, 0]))
                f(v.scalar_tensor_tensor(sg2[:], msk1[:], NEG, sgb[:],
                                         ALU.mult, ALU.add))
                f(v.tensor_reduce(m2[:, :, 0], sg2[:], AX.X, ALU.max))
                f(v.tensor_tensor(eq[:], sg2[:],
                                  m2[:].broadcast_to([128, NTC, NE]),
                                  ALU.is_ge))
                cur = eq
                for i, k in enumerate((1, 2, 4)):
                    nxt = cum[:, i % 2]
                    v.tensor_copy(nxt[:, :, 0:k], cur[:, :, 0:k])
                    ins = v.tensor_tensor(nxt[:, :, k:], cur[:, :, k:],
                                          cur[:, :, :NE - k], ALU.add)
                    f(ins)
                    cur = nxt
                f(v.tensor_sub(cum[:, 1], cur[:], eq[:]))
                f(v.tensor_single_scalar(cum[:, 0], cum[:, 1], 0.0,
                                         ALU.is_equal))
                f(v.tensor_mul(msk2[:], eq[:], cum[:, 0]))
                ins = v.tensor_sub(w12[:, 0, :], m1[:, :, 0], m2[:, :, 0])
                self.inc("v_g3", ins)
            block.vector(vtop1)

            def stop2(s):
                self.wait(s, "v_g3")
                s.activation(w12[:, 1, :], w12[:, 0, :], AF.Sigmoid,
                             scale=-1.0)
                ins = s.activation(w12[:, 0, :], w12[:, 0, :], AF.Sigmoid)
                self.inc("s_g", ins)
            block.scalar(stop2)

            def vtop2(v):
                self.wait(v, "s_g")
                f = lambda i: self.vfence(v, i)
                w1b = w12[:, 0, :].rearrange("p b -> p b 1") if False else None
                f(v.tensor_tensor(
                    cs["wEb"][:], msk1[:],
                    w12[:, 0:1, :].rearrange("p o b -> p b o")
                    .broadcast_to([128, NTC, NE]), ALU.mult))
                f(v.tensor_tensor(
                    sg2[:], msk2[:],
                    w12[:, 1:2, :].rearrange("p o b -> p b o")
                    .broadcast_to([128, NTC, NE]), ALU.mult))
                ins = v.tensor_add(cs["wEb"][:], cs["wEb"][:], sg2[:])
                self.inc("v_wEb", ins)
            block.vector(vtop2)

    def phase_d1(self, block, e):
        nc, d, cs = self.nc, self.d, self.cs
        P = "d1e%d_" % e
        with ExitStack() as st:
            w1h = self.sbuf(st, P + "w1h", [128, NDC, FF], BF16)
            w1l = self.sbuf(st, P + "w1l", [128, NDC, FF], BF16)
            wst = self.sbuf(st, P + "wst", [128, NDC, FF // 2], F32)
            xs_hi = self.sbuf(st, P + "x_hi", [128, NDC, 512], BF16)
            xs_lo = self.sbuf(st, P + "x_lo", [128, NDC, 512], BF16)
            hfull = self.sbuf(st, P + "hf", [128, 4, 512], F32)
            h_hi = self.sbuf(st, P + "hhi", [128, 4, 512], BF16)
            h_lo = self.sbuf(st, P + "hlo", [128, 4, 512], BF16)
            b1sb = self.sbuf(st, P + "b1", [128, NFC], F32)
            ps_h1 = self.psum(st, P + "ps", [128, 4, 512], F32)

            def syb(s, e=e):
                self.dinc("s_d1b", s.dma_start(out=b1sb[:], in_=d["b1c"][e]))
            block.sync(syb)

            for half in range(2):
                def syw(s, e=e, half=half):
                    self.wait(s, "v_d1w", self.val("v_d1w"))
                    self.dinc("s_d1w", s.dma_start(
                        out=wst[:],
                        in_=d["w1"][e][:, half * FF // 2:(half + 1) * FF // 2]
                            .rearrange("(c p) f -> p c f", p=128)))
                block.sync(syw)

                def vew(v, half=half):
                    self.wait(v, "s_d1w")
                    o0 = half * FF // 2
                    v.tensor_copy(w1h[:, :, o0:o0 + FF // 2], wst[:])
                    self.inc("v_d1w", v.tensor_sub(
                        w1l[:, :, o0:o0 + FF // 2], wst[:],
                        w1h[:, :, o0:o0 + FF // 2]))
                block.vector(vew)

            for ts in range(NTS):
                def sy1(s, e=e, ts=ts):
                    self.wait(s, "t_d1", self.val("t_d1"))
                    self.dinc("s_d1x", s.dma_start(
                        out=xs_hi[:],
                        in_=d["x2T_hi"][:, :, ts * 512:(ts + 1) * 512]
                            .rearrange("c p t -> p c t")))
                    self.dinc("s_d1x", s.dma_start(
                        out=xs_lo[:],
                        in_=d["x2T_lo"][:, :, ts * 512:(ts + 1) * 512]
                            .rearrange("c p t -> p c t")))
                block.sync(sy1)

                for fc in range(NFC):
                    g = ts * NFC + fc
                    sl = g % 4

                    def te1(t, e=e, ts=ts, fc=fc, sl=sl):
                        self.wait(t, "s_d1x")
                        self.wait(t, "v_d1w")
                        self.wait(t, "s_d1g", self.val("s_d1g") - 3)
                        ins = None
                        ops = [(w1h, xs_hi), (w1h, xs_lo), (w1l, xs_hi)]
                        for i, (ww, xx) in enumerate(ops):
                            for dc in range(NDC):
                                ins = t.matmul(
                                    ps_h1[:, sl],
                                    ww[:, dc, fc * 128:(fc + 1) * 128],
                                    xx[:, dc], start=(i == 0 and dc == 0),
                                    stop=(i == 2 and dc == NDC - 1))
                        self.inc("t_d1", ins)
                    block.tensor(te1)

                    def se1(s, e=e, fc=fc, sl=sl):
                        self.wait(s, "t_d1")
                        self.wait(s, "s_d1b")
                        self.wait(s, "v_d1s", self.val("v_d1s") - 3)
                        s.activation(hfull[:, sl], ps_h1[:, sl], AF.Gelu,
                                     bias=b1sb[:, fc:fc + 1])
                        ins = s.activation(h_hi[:, sl], hfull[:, sl], AF.Copy)
                        self.inc("s_d1g", ins)
                    block.scalar(se1)

                    def ve1(v, sl=sl):
                        self.wait(v, "s_d1g")
                        self.wait(v, "g_d1", self.val("g_d1") - 96)
                        ins = v.tensor_sub(h_lo[:, sl], hfull[:, sl],
                                           h_hi[:, sl])
                        self.inc("v_d1s", ins)
                    block.vector(ve1)

                    def ge1(g_, e=e, ts=ts, fc=fc, sl=sl):
                        self.wait(g_, "v_d1s")
                        self.dinc("g_d1", g_.dma_start(
                            out=d["hT_hi"][fc, :, ts * 512:(ts + 1) * 512],
                            in_=h_hi[:, sl]))
                        self.dinc("g_d1", g_.dma_start(
                            out=d["hT_lo"][fc, :, ts * 512:(ts + 1) * 512],
                            in_=h_lo[:, sl]))
                    block.gpsimd(ge1)

    def phase_d2(self, block, e):
        nc, d, cs = self.nc, self.d, self.cs
        P = "d2e%d_" % e
        with ExitStack() as st:
            w2h = self.sbuf(st, P + "w2h", [128, NFC, E], BF16)
            w2l = self.sbuf(st, P + "w2l", [128, NFC, E], BF16)
            wst = self.sbuf(st, P + "wst", [128, NFC, E // 2], F32)
            hsl_hi = self.sbuf(st, P + "h_hi", [128, 4, NFC, 128], BF16)
            hsl_lo = self.sbuf(st, P + "h_lo", [128, 4, NFC, 128], BF16)
            x2sl = self.sbuf(st, P + "x2", [128, 4, E], F32)
            b2sl = self.sbuf(st, P + "b2", [128, E], F32)
            gsl = self.sbuf(st, P + "g", [128, E], F32)
            blnsl = self.sbuf(st, P + "bln", [128, E], F32)
            rbuf = self.sbuf(st, P + "r", [128, E], F32)
            rsq = self.sbuf(st, P + "rsq", [128, E], F32)
            ybuf = self.sbuf(st, P + "y", [128, E], F32)
            contrib = self.sbuf(st, P + "ct", [128, 4, E], F32)
            cols = self.sbuf(st, P + "cols", [128, 8], F32)
            ps_o = self.psum(st, P + "ps", [128, 4, 2, 512], F32)

            def syb(s, e=e):
                self.dinc("s_d2b", s.dma_start(out=b2sl[:], in_=d["b2_rep"][e]))
                self.dinc("s_d2b", s.dma_start(out=gsl[:], in_=d["g_rep"][e]))
                self.dinc("s_d2b", s.dma_start(out=blnsl[:],
                                               in_=d["bln_rep"][e]))
            block.sync(syb)

            for half in range(2):
                def syw(s, e=e, half=half):
                    self.wait(s, "v_d2w", self.val("v_d2w"))
                    self.dinc("s_d2w", s.dma_start(
                        out=wst[:],
                        in_=d["w2"][e][:, half * E // 2:(half + 1) * E // 2]
                            .rearrange("(c p) o -> p c o", p=128)))
                block.sync(syw)

                def vew(v, half=half):
                    self.wait(v, "s_d2w")
                    o0 = half * E // 2
                    v.tensor_copy(w2h[:, :, o0:o0 + E // 2], wst[:])
                    self.inc("v_d2w", v.tensor_sub(
                        w2l[:, :, o0:o0 + E // 2], wst[:],
                        w2h[:, :, o0:o0 + E // 2]))
                block.vector(vew)

            for tc in range(NTC):
                sl = tc % 4

                def sy1(s, e=e, tc=tc, sl=sl):
                    self.wait(s, "t_d2", max(0, self.val("t_d2") - 2))
                    self.dinc("s_d2x", s.dma_start(
                        out=hsl_hi[:, sl],
                        in_=d["hT_hi"][:, :, tc * 128:(tc + 1) * 128]
                            .rearrange("f p t -> p f t")))
                    self.dinc("s_d2x", s.dma_start(
                        out=hsl_lo[:, sl],
                        in_=d["hT_lo"][:, :, tc * 128:(tc + 1) * 128]
                            .rearrange("f p t -> p f t")))
                    self.wait(s, "v_d2e", max(0, self.val("v_d2e") - 2))
                    self.dinc("s_d2x", s.dma_start(
                        out=x2sl[:, sl], in_=d["x2_d"][tc]))
                block.sync(sy1)

                def te1(t, e=e, tc=tc, sl=sl):
                    self.wait(t, "s_d2x")
                    self.wait(t, "v_d2w")
                    self.wait(t, "v_d2e", self.val("v_d2e") - 3)
                    ins = None
                    ops = [(hsl_hi, w2h), (hsl_hi, w2l), (hsl_lo, w2h)]
                    for ns in range(2):
                        for i, (hh, ww) in enumerate(ops):
                            for fc in range(NFC):
                                ins = t.matmul(
                                    ps_o[:, sl, ns], hh[:, sl, fc],
                                    ww[:, fc, ns * 512:(ns + 1) * 512],
                                    start=(i == 0 and fc == 0),
                                    stop=(i == 2 and fc == NFC - 1))
                    self.inc("t_d2", ins)
                block.tensor(te1)

                def ve1(v, e=e, tc=tc, sl=sl):
                    self.wait(v, "t_d2")
                    self.wait(v, "s_d2b")
                    f = lambda i: self.vfence(v, i)
                    psv = ps_o[:, sl].rearrange("p a b -> p (a b)")
                    v.tensor_add(rbuf[:], psv, x2sl[:, sl])
                    f(v.tensor_add(rbuf[:], rbuf[:], b2sl[:]))
                    v.tensor_mul(rsq[:], rbuf[:], rbuf[:])
                    v.tensor_reduce(cols[:, 0:1], rbuf[:], AX.X, ALU.add)
                    f(v.tensor_reduce(cols[:, 1:2], rsq[:], AX.X, ALU.add))
                    f(v.tensor_scalar_mul(cols[:, 0:1], cols[:, 0:1], 1.0 / E))
                    f(v.tensor_scalar_mul(cols[:, 1:2], cols[:, 1:2], 1.0 / E))
                    f(v.tensor_mul(cols[:, 2:3], cols[:, 0:1], cols[:, 0:1]))
                    f(v.tensor_sub(cols[:, 1:2], cols[:, 1:2], cols[:, 2:3]))
                    ins = v.tensor_scalar_add(cols[:, 1:2], cols[:, 1:2],
                                              EPS_FFN)
                    self.inc("v_ln2", ins)
                block.vector(ve1)

                def se1(s, tc=tc):
                    self.wait(s, "v_ln2")
                    ins = s.activation(cols[:, 3:4], cols[:, 1:2], AF.Sqrt)
                    self.inc("s_ln2", ins)
                block.scalar(se1)

                def ve2(v, e=e, tc=tc, sl=sl):
                    self.wait(v, "s_ln2")
                    self.wait(v, "v_wEb")
                    f = lambda i: self.vfence(v, i)
                    f(v.reciprocal(cols[:, 4:5], cols[:, 3:4]))
                    f(v.tensor_mul(cols[:, 5:6], cols[:, 4:5], cols[:, 4:5]))
                    f(v.tensor_mul(cols[:, 5:6], cols[:, 5:6], cols[:, 1:2]))
                    f(v.tensor_scalar(cols[:, 5:6], cols[:, 5:6], -0.5, 1.5,
                                      ALU.mult, ALU.add))
                    f(v.tensor_mul(cols[:, 4:5], cols[:, 4:5], cols[:, 5:6]))
                    f(v.tensor_mul(cols[:, 6:7], cols[:, 0:1], cols[:, 4:5]))
                    f(v.tensor_scalar_mul(cols[:, 6:7], cols[:, 6:7], -1.0))
                    v.tensor_scalar(ybuf[:], rbuf[:], cols[:, 4:5],
                                    cols[:, 6:7], ALU.mult, ALU.add)
                    v.tensor_mul(ybuf[:], ybuf[:], gsl[:])
                    v.tensor_add(ybuf[:], ybuf[:], blnsl[:])
                    self.wait(v, "g_d2", self.val("g_d2") - 48)
                    ins = v.tensor_scalar_mul(contrib[:, sl], ybuf[:],
                                              cs["wEb"][:, tc, e:e + 1])
                    self.inc("v_d2e", ins)
                block.vector(ve2)

                def ge1(g_, e=e, tc=tc, sl=sl):
                    self.wait(g_, "v_d2e")
                    if e == 0:
                        ins = g_.dma_start(out=d["acc_d"][tc],
                                           in_=contrib[:, sl])
                    else:
                        ins = g_.dma_start(out=d["acc_d"][tc],
                                           in_=contrib[:, sl],
                                           accum_op=ALU.add)
                    self.dinc("g_d2", ins)
                block.gpsimd(ge1)

    def phase_e(self, block):
        nc, d, cs = self.nc, self.d, self.cs
        with ExitStack() as st:
            acc = self.sbuf(st, "e_acc", [128, 2, E], F32)
            x2sl = self.sbuf(st, "e_x2", [128, 2, E], F32)
            rbuf = self.sbuf(st, "e_r", [128, E], F32)
            rsq = self.sbuf(st, "e_rsq", [128, E], F32)
            obuf = self.sbuf(st, "e_o", [128, 2, E], F32)
            cols = self.sbuf(st, "e_cols", [128, 8], F32)

            for tc in range(NTC):
                sl = tc % 2

                def sy1(s, tc=tc, sl=sl):
                    self.wait(s, "v_e", self.val("v_e") - 1)
                    self.dinc("s_e", s.dma_start(out=acc[:, sl],
                                                 in_=d["acc_d"][tc]))
                    self.dinc("s_e", s.dma_start(out=x2sl[:, sl],
                                                 in_=d["x2_d"][tc]))
                block.sync(sy1)

                def ve1(v, tc=tc, sl=sl):
                    self.wait(v, "s_e")
                    f = lambda i: self.vfence(v, i)
                    v.tensor_add(rbuf[:], acc[:, sl], x2sl[:, sl])
                    v.tensor_mul(rsq[:], rbuf[:], rbuf[:])
                    v.tensor_reduce(cols[:, 0:1], rbuf[:], AX.X, ALU.add)
                    f(v.tensor_reduce(cols[:, 1:2], rsq[:], AX.X, ALU.add))
                    f(v.tensor_scalar_mul(cols[:, 0:1], cols[:, 0:1], 1.0 / E))
                    f(v.tensor_scalar_mul(cols[:, 1:2], cols[:, 1:2], 1.0 / E))
                    f(v.tensor_mul(cols[:, 2:3], cols[:, 0:1], cols[:, 0:1]))
                    f(v.tensor_sub(cols[:, 1:2], cols[:, 1:2], cols[:, 2:3]))
                    ins = v.tensor_scalar_add(cols[:, 1:2], cols[:, 1:2],
                                              EPS_OUT)
                    self.inc("v_eln", ins)
                block.vector(ve1)

                def se1(s, tc=tc):
                    self.wait(s, "v_eln")
                    ins = s.activation(cols[:, 3:4], cols[:, 1:2], AF.Sqrt)
                    self.inc("s_eln", ins)
                block.scalar(se1)

                def ve2(v, tc=tc, sl=sl):
                    self.wait(v, "s_eln")
                    f = lambda i: self.vfence(v, i)
                    f(v.reciprocal(cols[:, 4:5], cols[:, 3:4]))
                    f(v.tensor_mul(cols[:, 5:6], cols[:, 4:5], cols[:, 4:5]))
                    f(v.tensor_mul(cols[:, 5:6], cols[:, 5:6], cols[:, 1:2]))
                    f(v.tensor_scalar(cols[:, 5:6], cols[:, 5:6], -0.5, 1.5,
                                      ALU.mult, ALU.add))
                    f(v.tensor_mul(cols[:, 4:5], cols[:, 4:5], cols[:, 5:6]))
                    f(v.tensor_mul(cols[:, 6:7], cols[:, 0:1], cols[:, 4:5]))
                    f(v.tensor_scalar_mul(cols[:, 6:7], cols[:, 6:7], -1.0))
                    self.wait(v, "g_out", self.val("g_out") - 16)
                    ins = v.tensor_scalar(obuf[:, sl], rbuf[:], cols[:, 4:5],
                                          cols[:, 6:7], ALU.mult, ALU.add)
                    self.inc("v_e", ins)
                block.vector(ve2)

                def ge1(g_, tc=tc, sl=sl):
                    self.wait(g_, "v_e")
                    self.dinc("g_out", g_.dma_start(
                        out=d["out"][tc * 128:(tc + 1) * 128, :],
                        in_=obuf[:, sl]))
                block.gpsimd(ge1)

    def finalize(self, block):
        self.barrier(block)


def host_prepare(inp):
    x = np.asarray(inp["L0_attn_x"], dtype=np.float32)
    base = {
        "wq": np.ascontiguousarray(inp["L0_attn_Wq"], np.float32),
        "wk": np.ascontiguousarray(inp["L0_attn_Wk"], np.float32),
        "wv": np.ascontiguousarray(inp["L0_attn_Wv"], np.float32),
        "wo": np.ascontiguousarray(inp["L0_attn_Wo"], np.float32),
        "wdown": np.ascontiguousarray(inp["L0_attn_Wdown"], np.float32),
        "wgates": np.ascontiguousarray(
            np.concatenate([inp["L0_attn_Wig"], inp["L0_attn_Wfg"]], axis=1),
            np.float32),
        "bgates": np.ascontiguousarray(
            np.concatenate([inp["L0_attn_big"], inp["L0_attn_bfg"]])[:, None],
            np.float32),
        "normw_rep": np.ascontiguousarray(np.broadcast_to(
            np.asarray(inp["L0_attn_norm_w"], np.float32).reshape(1, E), (128, E))),
        "gate_w": np.ascontiguousarray(inp["L0_ffn_gate_w"], np.float32),
        "w1": np.ascontiguousarray(inp["L0_ffn_w1"], np.float32),
        "b1c": np.ascontiguousarray(
            np.asarray(inp["L0_ffn_b1"], np.float32).reshape(NE, NFC, 128)
            .transpose(0, 2, 1)),
        "w2": np.ascontiguousarray(inp["L0_ffn_w2"], np.float32),
        "b2_rep": np.ascontiguousarray(np.broadcast_to(
            np.asarray(inp["L0_ffn_b2"], np.float32)[:, None, :], (NE, 128, E))),
        "g_rep": np.ascontiguousarray(np.broadcast_to(
            np.asarray(inp["L0_ffn_ln_g"], np.float32)[:, None, :], (NE, 128, E))),
        "bln_rep": np.ascontiguousarray(np.broadcast_to(
            np.asarray(inp["L0_ffn_ln_b"], np.float32)[:, None, :], (NE, 128, E))),
        "identf": np.eye(128, dtype=np.float32),
        "identb": np.eye(128, dtype=np.float32).astype(ml_dtypes.bfloat16),
        "tri": np.triu(np.ones((128, 128), np.float32)),
        "ones_row": np.ones((1, 128), np.float32),
        "ones_colb": np.ones((128, 1), np.float32).astype(ml_dtypes.bfloat16),
    }
    in_maps = []
    for c in range(8):
        m = dict(base)
        m["x"] = np.ascontiguousarray(x[c])
        in_maps.append(m)
    return in_maps


def run(debug_outs=(), phases="A", n_cores=1, inputs=None):
    mk = MK(debug_outs=debug_outs, phases=phases)
    nc = mk.build()
    in_maps = host_prepare(inputs)[:n_cores]
    res = run_bass_kernel_spmd(nc, in_maps, core_ids=list(range(n_cores)))
    return res.results


def kernel(**inputs):
    mk = MK(debug_outs=(), phases="ABCDE")
    nc = mk.build()
    in_maps = host_prepare(inputs)
    res = run_bass_kernel_spmd(nc, in_maps, core_ids=list(range(8)))
    return np.stack([res.results[c]["out"] for c in range(8)], axis=0)
